# version 23
# baseline (speedup 1.0000x reference)
import numpy as np
import ml_dtypes

import concourse.bass as bass
import concourse.mybir as mybir
from concourse.bass_utils import run_bass_kernel_spmd

F32 = mybir.dt.float32
BF16 = mybir.dt.bfloat16
AF = mybir.ActivationFunctionType
ALU = mybir.AluOpType
AX = mybir.AxisListType

H = 2048
NH = 16
HD = 128
DH = 8192
NC = H // 128
HC = DH // 128
SCALE = float(HD) ** -0.5
NEG_BIG = -1.0e30

ENGINES = ("pe", "act", "dve", "sync", "pool")
DMA_ENG = {"sync", "pool", "act"}


class Buf:
    __slots__ = ("name", "writer", "readers")

    def __init__(self, name):
        self.name = name
        self.writer = None
        self.readers = []


class Emitter:
    RING = 12

    def __init__(self):
        self.ops = {e: [] for e in ENGINES}
        self.queue = []
        self.qhead = 0

    def op(self, eng, fn, reads=(), writes=(), tag="", is_dma=False):
        idx = len(self.ops[eng])
        deps = set()
        for b in reads:
            if b.writer is not None:
                deps.add(b.writer)
        for b in writes:
            if b.writer is not None:
                deps.add(b.writer)
            for r in b.readers:
                deps.add(r)
        me = (eng, idx)
        for b in writes:
            b.writer = me
            b.readers = []
        for b in reads:
            b.readers.append(me)
        deps.discard(me)
        if eng == "pe":
            deps = {d for d in deps if d[0] != "pe"}
        self.ops[eng].append([fn, deps, False, tag, is_dma])

    def fence(self, target_buf, source_bufs):
        for s in source_bufs:
            if s.writer is not None:
                target_buf.readers.append(s.writer)
            target_buf.readers.extend(s.readers)

    def push(self, fn):
        self.queue.append(fn)

    def pull(self, n):
        while n > 0 and self.qhead < len(self.queue):
            self.queue[self.qhead]()
            self.qhead += 1
            n -= 1

    def drain(self):
        self.pull(len(self.queue) - self.qhead)

    def resolve_and_emit(self, nc, block):
        RING = self.RING
        assert self.qhead == len(self.queue), "filler queue not drained"
        for e in ENGINES:
            for rec in self.ops[e]:
                for (pe_, pi) in rec[1]:
                    self.ops[pe_][pi][2] = True
                if rec[4]:
                    rec[2] = True
        key, val, dmai = {}, {}, {}
        for e in ENGINES:
            d = 0
            c = 0
            for i, rec in enumerate(self.ops[e]):
                if rec[4]:
                    key[(e, i)] = (e, d % RING)
                    val[(e, i)] = 16 * (d // RING + 1)
                    dmai[(e, i)] = d
                    d += 1
                else:
                    if rec[2]:
                        c += 1
                    key[(e, i)] = e
                    val[(e, i)] = c
        semkeys = list(ENGINES)
        semkeys += [(e, s) for e in DMA_ENG for s in range(RING)]
        sems = {k: nc.alloc_semaphore(f"sem_{k}") for k in semkeys}

        def emit_stream(eng_obj, e):
            hw = {}
            n_ops = len(self.ops[e])
            for i, rec in enumerate(self.ops[e]):
                fn, deps, need_inc, tag, is_dma = rec
                per_sem = {}
                for dep in deps:
                    k = key[dep]
                    v = val[dep]
                    if v > per_sem.get(k, 0):
                        per_sem[k] = v
                if is_dma:
                    d = dmai[(e, i)]
                    if d >= RING:
                        k = (e, d % RING)
                        v = 16 * (d // RING)
                        if v > per_sem.get(k, 0):
                            per_sem[k] = v
                for k, v in per_sem.items():
                    if v > hw.get(k, 0):
                        eng_obj.wait_ge(sems[k], v)
                        hw[k] = v
                inst = fn(eng_obj)
                if need_inc:
                    inst.then_inc(sems[key[(e, i)]], 16 if is_dma else 1)
            if e in DMA_ENG and n_ops:
                last = {}
                for i in range(n_ops):
                    k = key[(e, i)]
                    if isinstance(k, tuple):
                        last[k] = val[(e, i)]
                for k, v in sorted(last.items()):
                    eng_obj.wait_ge(sems[k], v)

        @block.tensor
        def _(t):
            emit_stream(t, "pe")

        @block.scalar
        def _(s):
            emit_stream(s, "act")

        @block.vector
        def _(v):
            emit_stream(v, "dve")

        @block.sync
        def _(s):
            emit_stream(s, "sync")

        @block.gpsimd
        def _(g):
            emit_stream(g, "pool")


def build_program(B_loc=2, S=2048):
    G = S // 512
    nc = bass.Bass("TRN2", target_bir_lowering=False)

    x_fm = nc.dram_tensor("x_fm", [B_loc, H, S], F32, kind="ExternalInput").ap()
    wqkv = nc.dram_tensor("wqkv", [H, H + 2 * HD], F32, kind="ExternalInput").ap()
    wproj = nc.dram_tensor("wproj", [H, H], F32, kind="ExternalInput").ap()
    upT = nc.dram_tensor("upT", [H, DH], F32, kind="ExternalInput").ap()
    dnT = nc.dram_tensor("dnT", [DH, H], F32, kind="ExternalInput").ap()
    bqkv_t = nc.dram_tensor("bqkv_t", [128, 18], F32, kind="ExternalInput").ap()
    bv_row = nc.dram_tensor("bv_row", [1, 128], F32, kind="ExternalInput").ap()
    bproj_t = nc.dram_tensor("bproj_t", [128, NC], F32, kind="ExternalInput").ap()
    us_t = nc.dram_tensor("us_t", [128, HC], F32, kind="ExternalInput").ap()
    ds_t = nc.dram_tensor("ds_t", [128, NC], F32, kind="ExternalInput").ap()
    os_col = nc.dram_tensor("os_col", [128, 1], F32, kind="ExternalInput").ap()
    utri_in = nc.dram_tensor("utri_in", [128, 128], BF16, kind="ExternalInput").ap()
    negi_in = nc.dram_tensor("negi_in", [128, 128], BF16, kind="ExternalInput").ap()
    onesc_in = nc.dram_tensor("onesc_in", [128, 1], BF16, kind="ExternalInput").ap()
    onescf_in = nc.dram_tensor("onescf_in", [128, 1], F32, kind="ExternalInput").ap()
    onesr_in = nc.dram_tensor("onesr_in", [1, 128], BF16, kind="ExternalInput").ap()
    onesrf_in = nc.dram_tensor("onesrf_in", [1, 128], F32, kind="ExternalInput").ap()

    y_fm = nc.dram_tensor("y_fm", [B_loc, H, S], F32, kind="ExternalOutput").ap()

    q_dram = nc.dram_tensor("q_dram", [B_loc, NH, G, 128, 512], BF16, kind="Internal").ap()
    up_scr = nc.dram_tensor("up_scr", [DH, H], BF16, kind="Internal").ap()
    dn_scr = nc.dram_tensor("dn_scr", [NC, 2, 128, 32 * 128], BF16, kind="Internal").ap()
    wp_scr = nc.dram_tensor("wp_scr", [NC, 128, H], BF16, kind="Internal").ap()

    A = nc.alloc_sbuf_tensor
    big_bf = A("big_bf", [128, 16 * 2048], BF16)
    qt_buf = [A(f"qt{i}", [128, 512], BF16) for i in range(2)]
    q_stage = [A(f"qst{i}", [128, 512], BF16) for i in range(2)]
    k_sb = A("k_sb", [128, S], BF16)
    vtok_sb = A("vtok_sb", [128, S], BF16)
    attn_sb = A("attn_sb", [128, NC * 512], BF16)
    ypre_bf = A("ypre_bf", [128, NC * 512], BF16)
    v2_sb = A("v2_sb", [128, NC * 512], F32)
    probsu_sb = A("probsu", [128, 16 * 512], BF16)
    tmp_f = [A(f"tmpf{i}", [128, 512], F32) for i in range(2)]
    cB_sb = [A(f"cB{i}", [128, 512], F32) for i in range(2)]
    rB_sb = A("rB_sb", [128, 512], F32)
    crow_sb = A("crow", [1, 512], BF16)
    q2t_sb = A("q2t", [128, 512], BF16)
    wq_buf = [A(f"wq{i}", [128, 16 * 128], BF16) for i in range(2)]
    dn_buf = [A(f"dn{i}", [128, 32 * 128], BF16) for i in range(2)]
    xr_buf = [A(f"xr{i}", [128, 512], F32) for i in range(2)]
    kmax2_sb = A("kmax2", [1, 1], F32)
    nmean_sb = A("nmean", [1, 512], BF16)
    rows_sb = A("rows", [1, 3 * 512], F32)
    _rowA = 512
    _rowB = 1024
    bqkv_sb = A("bqkv_sb", [128, 18], F32)
    bqs_sb = A("bqs_sb", [128, 16], F32)
    bvr_sb = A("bvr", [1, 128], BF16)
    bproj_sb = A("bproj_sb", [128, NC], F32)
    us_sb = A("us_sb", [128, HC], F32)
    ds_sb = A("ds_sb", [128, NC], F32)
    os_sb = A("os_sb", [128, 1], F32)
    csc_sb = A("csc_sb", [128, NC], F32)
    utri_sb = A("utri", [128, 128], BF16)
    negi_sb = A("negi", [128, 128], BF16)
    onesc_sb = A("onesc", [128, 1], BF16)
    onescf_sb = A("onescf", [128, 1], F32)
    onesr_sb = A("onesr", [1, 128], BF16)
    onesrf_sb = A("onesrf", [1, 128], F32)

    def addr_of(h):
        return nc.lookup_mloc(h).addr

    v2_addr = addr_of(v2_sb)
    k2_sb = nc.alloc_sbuf_tensor_at("k2_sb", [128, 2048], BF16, offset=v2_addr + 16384)
    kn2_sb = nc.alloc_sbuf_tensor_at("kn2_sb", [1, 2048], BF16, offset=v2_addr + 20480)
    yp_addr = addr_of(ypre_bf)
    cvt_f = [nc.alloc_sbuf_tensor_at(f"cvtf{i}", [128, 2048], F32, offset=v2_addr + i * 8192)
             for i in range(2)]
    pu_addr = addr_of(probsu_sb)
    dnstage = nc.alloc_sbuf_tensor_at("dnstage", [128, 4096], F32, offset=pu_addr)
    cvt_b = [nc.alloc_sbuf_tensor_at(f"cvtb{i}", [128, 2048], BF16, offset=yp_addr + i * 4096)
             for i in range(2)]
    wp_buf = [nc.alloc_sbuf_tensor_at(f"wp{i}", [128, 16 * 128], BF16, offset=addr_of(wq_buf[i]))
              for i in range(2)]
    v2sq_buf = [nc.alloc_sbuf_tensor_at(f"v2sq{i}", [128, 512], BF16,
                                        offset=addr_of(tmp_f[0]) + i * 1024)
                for i in range(2)]
    rem = nc.sbuf_bytes_remaining
    assert rem >= 0, rem

    PB = [nc.alloc_psum_tensor(f"PB{i}", [128, 512], F32) for i in range(8)]

    em = Emitter()
    B_ = Buf
    b_big = B_("big")
    b_cvtf = [B_("cvtf0"), B_("cvtf1")]
    b_cvtb = [B_("cvtb0"), B_("cvtb1")]
    b_qt = [B_("qt0"), B_("qt1")]
    b_qst = [B_("qst0"), B_("qst1")]
    b_k = B_("k")
    b_vtok = B_("vtok")
    b_attn = B_("attn")
    b_ypre = B_("ypre")
    b_v2 = B_("v2")
    b_pu = [B_(f"pu{c}") for c in range(16)]
    b_k2 = B_("k2")
    b_kn2 = B_("kn2")
    b_tmp = [B_("tmp0"), B_("tmp1")]
    b_cB = [B_("cB0"), B_("cB1")]
    b_rB = B_("rB")
    b_q2t = B_("q2t")
    b_wq = [B_("wqup0"), B_("wqup1")]
    b_dn = [B_("wpdn0"), B_("wpdn1")]
    b_xr = [B_("xr0"), B_("xr1")]
    b_v2sq = b_tmp[0]
    b_dnstage = B_("dnstage")
    b_kmax = B_("kmax")
    b_nmean = B_("nmean")
    b_rows = B_("rows")
    b_tabs = B_("tabs")
    b_csc = B_("csc")
    b_bqs = B_("bqs")
    b_PB = [B_(f"PB{i}") for i in range(8)]
    b_qdram = B_("qdram")
    b_upscr = B_("upscr")
    b_dnscr = B_("dnscr")
    b_wpscr = B_("wpscr")
    b_out = B_("out")

    def dma(queue, dst, src, reads, writes, tag=""):
        em.op(queue, lambda e, d=dst, s=src: e.dma_start(out=d, in_=s),
              reads=reads, writes=writes, tag=tag, is_dma=True)

    def mm(out_ap, lhs_ap, rhs_ap, start, stop, reads, writes, tag=""):
        em.op("pe",
              lambda e, o=out_ap, l=lhs_ap, r=rhs_ap, st=start, sp=stop:
                  e.matmul(o, l, r, start=st, stop=sp),
              reads=reads, writes=writes, tag=tag)

    for dst, src in ((bqkv_sb, bqkv_t), (bproj_sb, bproj_t), (us_sb, us_t),
                     (ds_sb, ds_t), (os_sb, os_col), (utri_sb, utri_in),
                     (negi_sb, negi_in), (onesc_sb, onesc_in), (onescf_sb, onescf_in),
                     (onesr_sb, onesr_in), (onesrf_sb, onesrf_in)):
        dma("sync", dst[:], src, reads=(), writes=(b_tabs,), tag="consts")
    dma("pool", bvr_sb[:], bv_row, reads=(), writes=(b_tabs,), tag="bvr")
    em.op("dve", lambda e: e.tensor_single_scalar(csc_sb[:], ds_sb[:], os_sb[:, 0:1], ALU.mult),
          reads=(b_tabs,), writes=(b_csc,), tag="csc")
    em.op("dve", lambda e: e.tensor_single_scalar(bqs_sb[:], bqkv_sb[:, 0:16], float(SCALE), ALU.mult),
          reads=(b_tabs,), writes=(b_bqs,), tag="bqs")

    def emit_x_loads(b):
        for dc in range(NC):
            rd = (b_dn[0], b_dn[1]) if (b > 0 and dc == 0) else ()
            dma("pool", big_bf[:, S * dc:S * (dc + 1)],
                x_fm[b, 128 * dc:128 * (dc + 1), 0:S],
                reads=rd, writes=(b_big,), tag=f"xbf{b}_{dc}")

    def emit_wqkv_load(b, j):
        r = j % 2
        src = wqkv[:, 128 * j:128 * (j + 1)].rearrange("(dc p) j -> p dc j", p=128)
        dma("pool", wq_buf[r][:].rearrange("p (dc j) -> p dc j", j=128), src,
            reads=(), writes=(b_wq[r],), tag=f"wqkv{b}_{j}")

    emit_x_loads(0)
    emit_wqkv_load(0, 16)
    emit_wqkv_load(0, 17)

    def qkv_head_group(b, j):
        r = j % 2
        for tb in range(G):
            pr = 6 + (j * G + tb) % 2
            for dc in range(NC):
                mm(PB[pr][:], wq_buf[r][:, 128 * dc:128 * (dc + 1)],
                   big_bf[:, S * dc + 512 * tb:S * dc + 512 * (tb + 1)],
                   start=(dc == 0), stop=(dc == NC - 1),
                   reads=(b_wq[r], b_big), writes=(b_PB[pr],), tag=f"qkv{j}")
            if j < 16:
                sr = (j * G + tb) % 2
                em.op("dve",
                      lambda e, pr=pr, sr=sr, j=j: e.tensor_scalar(
                          q_stage[sr][:], PB[pr][:], float(SCALE), bqs_sb[:, j:j + 1],
                          op0=ALU.mult, op1=ALU.add),
                      reads=(b_PB[pr], b_bqs), writes=(b_qst[sr],), tag="q_evac")
                dma("act", q_dram[b, j, tb], q_stage[sr][:],
                    reads=(b_qst[sr],), writes=(b_qdram,), tag="q_st")
            else:
                em.op("dve",
                      lambda e, pr=pr, tb=tb: e.tensor_scalar(
                          k_sb[:, 512 * tb:512 * (tb + 1)], PB[pr][:],
                          bqkv_sb[:, 16:17], None, op0=ALU.add),
                      reads=(b_PB[pr], b_tabs), writes=(b_k,), tag="k_evac")

    def vtok_group(b):
        r = 17 % 2
        for c in range(S // 128):
            for dc in range(NC):
                mm(PB[2][:, 0:128],
                   big_bf[:, S * dc + 128 * c:S * dc + 128 * (c + 1)],
                   wq_buf[r][:, 128 * dc:128 * (dc + 1)],
                   start=(dc == 0), stop=False,
                   reads=(b_big, b_wq[r]), writes=(b_PB[2],), tag="vtok")
            mm(PB[2][:, 0:128], onesr_sb[:], bvr_sb[0:1, :],
               start=False, stop=True, reads=(b_tabs,), writes=(b_PB[2],), tag="vtok_b")
            em.op("act",
                  lambda e, c=c: e.activation(
                      out=vtok_sb[:, 128 * c:128 * (c + 1)], in_=PB[2][:, 0:128],
                      func=AF.Copy),
                  reads=(b_PB[2],), writes=(b_vtok,), tag="vtok_evac")

    def k_stats(b):
        em.fence(b_k2, [b_v2])
        em.fence(b_kn2, [b_v2])
        em.op("act", lambda e: e.activation(out=k2_sb[:, 0:S], in_=k_sb[:, 0:S], func=AF.Square),
              reads=(b_k,), writes=(b_k2,), tag="k2")
        for t4 in range(G):
            mm(PB[3][0:1, :], onesc_sb[:], k2_sb[:, 512 * t4:512 * (t4 + 1)],
               start=True, stop=True, reads=(b_k2, b_tabs), writes=(b_PB[3],), tag="kn2mm")
            em.op("act",
                  lambda e, t4=t4: e.activation(
                      out=kn2_sb[0:1, 512 * t4:512 * (t4 + 1)], in_=PB[3][0:1, :], func=AF.Copy),
                  reads=(b_PB[3],), writes=(b_kn2,), tag="kn2_evac")
        em.op("dve", lambda e: e.reduce_max(kmax2_sb[0:1, 0:1], kn2_sb[0:1, 0:S], axis=AX.X),
              reads=(b_kn2,), writes=(b_kmax,), tag="kmax2")

    def p2head(b, g, h, light=False, budget=8):
        C = 4 * g + 4
        bud = [1 if light else budget]

        def pull_b(n):
            n = min(n, bud[0])
            if n > 0:
                em.pull(n)
                bud[0] -= n
        qr = h % 2
        cr = h % 2
        if h == 0:
            dma("sync", qt_buf[0][:], q_dram[b, 0, g],
                reads=(b_qdram,), writes=(b_qt[0],), tag="q_ld")
        if h + 1 < NH:
            dma("sync", qt_buf[(h + 1) % 2][:], q_dram[b, h + 1, g],
                reads=(b_qdram,), writes=(b_qt[(h + 1) % 2],), tag="q_ld")
        em.op("act", lambda e, qr=qr: e.activation(out=q2t_sb[:], in_=qt_buf[qr][:], func=AF.Square),
              reads=(b_qt[qr],), writes=(b_q2t,), tag="q2")
        mm(PB[3][32:33, :], onesc_sb[:], q2t_sb[:], start=True, stop=True,
           reads=(b_q2t, b_tabs), writes=(b_PB[3],), tag="q2sum")
        em.op("act",
              lambda e: e.activation(out=crow_sb[0:1, :], in_=PB[3][32:33, :],
                                     func=AF.Sqrt, scale=kmax2_sb[0:1, 0:1]),
              reads=(b_PB[3], b_kmax), writes=(b_rows,), tag="crow")
        mm(PB[3][:], onesr_sb[:], crow_sb[0:1, :], start=True, stop=True,
           reads=(b_rows, b_tabs), writes=(b_PB[3],), tag="cB_mm")
        em.op("dve", lambda e, cr=cr: e.tensor_copy(out=cB_sb[cr][:], in_=PB[3][:]),
              reads=(b_PB[3],), writes=(b_cB[cr],), tag="cB_cp")
        pull_b(2)

        def pv_s(c):
            col0 = 128 * (c - 4 * g) if c >= 4 * g else 0
            mm(PB[2][:, col0:512], vtok_sb[:, 128 * c:128 * (c + 1)],
               probsu_sb[:, 512 * c + col0:512 * (c + 1)],
               start=(c == 0), stop=(c == C - 1),
               reads=(b_vtok, b_pu[c]), writes=(b_PB[2],), tag="pv")
            mm(PB[3][0:1, col0:512], onesc_sb[:],
               probsu_sb[:, 512 * c + col0:512 * (c + 1)],
               start=(c == 0), stop=(c == C - 1),
               reads=(b_pu[c], b_tabs), writes=(b_PB[3],), tag="srow")

        LAG = 4
        for c in range(C):
            col0 = 128 * (c - 4 * g) if c >= 4 * g else 0
            NP = 512 - col0
            pr = (0, 1, 5)[c % 3]
            diag = c >= 4 * g
            mm(PB[pr][:, 0:NP], k_sb[:, 128 * c:128 * (c + 1)],
               qt_buf[qr][:, col0:512],
               start=True, stop=(not diag),
               reads=(b_k, b_qt[qr]), writes=(b_PB[pr],), tag="qk")
            if diag:
                mm(PB[pr][:, 0:128], utri_sb[:], negi_sb[:],
                   start=False, stop=True, reads=(b_tabs,), writes=(b_PB[pr],), tag="mask")
            tr = c % 2
            em.op("dve",
                  lambda e, pr=pr, tr=tr, NP=NP, col0=col0, cr=cr: e.tensor_tensor(
                      tmp_f[tr][:, 0:NP], PB[pr][:, 0:NP], cB_sb[cr][:, col0:512],
                      op=ALU.subtract),
                  reads=(b_PB[pr], b_cB[cr]), writes=(b_tmp[tr],), tag="subc")
            em.op("act",
                  lambda e, tr=tr, NP=NP, c=c, col0=col0: e.activation(
                      out=probsu_sb[:, 512 * c + col0:512 * (c + 1)],
                      in_=tmp_f[tr][:, 0:NP], func=AF.Exp),
                  reads=(b_tmp[tr],), writes=(b_pu[c],), tag="exp")
            if c >= LAG:
                pv_s(c - LAG)
            if c % 2 == 1 and bud[0] > 2:
                em.pull(1)
                bud[0] -= 1
        for c in range(max(0, C - LAG), C):
            pv_s(c)
        em.op("dve", lambda e: e.reciprocal(rows_sb[0:1, _rowA:_rowA + 512], PB[3][0:1, :]),
              reads=(b_PB[3],), writes=(b_rows,), tag="recip")
        pull_b(2)
        mm(PB[3][:], onesrf_sb[:], rows_sb[0:1, _rowA:_rowA + 512], start=True, stop=True,
           reads=(b_rows, b_tabs), writes=(b_PB[3],), tag="rB_mm")
        em.op("dve", lambda e: e.tensor_copy(out=rB_sb[:], in_=PB[3][:]),
              reads=(b_PB[3],), writes=(b_rB,), tag="rB_cp")
        em.op("dve",
              lambda e, h=h: e.tensor_tensor(
                  attn_sb[:, 512 * h:512 * (h + 1)], PB[2][:], rB_sb[:], op=ALU.mult),
              reads=(b_PB[2], b_rB), writes=(b_attn,), tag="attn_evac")

    def p1_with_p2g0(b):
        em.drain()
        if b > 0:
            emit_x_loads(b)
            emit_wqkv_load(b, 16)
            emit_wqkv_load(b, 17)
        qkv_head_group(b, 16)
        k_stats(b)
        vtok_group(b)
        emit_wqkv_load(b, 0)
        emit_wqkv_load(b, 1)

        def qkv_closure(j):
            qkv_head_group(b, j)
            if j + 2 < 16:
                emit_wqkv_load(b, j + 2)

        for j in range(16):
            em.push(lambda j=j: qkv_closure(j))
        for h in range(NH):
            em.pull(2 if h == 0 else 1)
            p2head(b, 0, h, light=True)
        em.drain()

    def p3(b, g):
        dma("sync", xr_buf[0][:], x_fm[b, 0:128, 512 * g:512 * (g + 1)],
            reads=(), writes=(b_xr[0],), tag="x_ld")
        first = (b == 0 and g == 0)

        def wp_load(dout):
            r = dout % 2
            if first:
                s2 = wproj[:, 128 * dout:128 * (dout + 1)].rearrange("(dc p) j -> p dc j", p=128)
                dma("pool", wp_buf[r][:].rearrange("p (dc j) -> p dc j", j=128), s2,
                    reads=(), writes=(b_wq[r],), tag="wp_cvld")
                dma("pool", wp_scr[dout], wp_buf[r][:],
                    reads=(b_wq[r],), writes=(b_wpscr,), tag="wp_cvst")
            else:
                dma("sync", wp_buf[r][:], wp_scr[dout],
                    reads=(b_wpscr,), writes=(b_wq[r],), tag="wp_ld")

        wp_load(0)
        wp_load(1)
        for dout in range(NC):
            r = dout % 2
            if dout + 1 < NC:
                dma("sync", xr_buf[(dout + 1) % 2][:],
                    x_fm[b, 128 * (dout + 1):128 * (dout + 2), 512 * g:512 * (g + 1)],
                    reads=(), writes=(b_xr[(dout + 1) % 2],), tag="x_ld")
            pr = 6 + dout % 2
            for dc in range(NC):
                mm(PB[pr][:], wp_buf[r][:, 128 * dc:128 * (dc + 1)],
                   attn_sb[:, 512 * dc:512 * (dc + 1)],
                   start=(dc == 0), stop=(dc == NC - 1),
                   reads=(b_wq[r], b_attn), writes=(b_PB[pr],), tag="proj")
            if dout + 2 < NC:
                wp_load(dout + 2)
            em.op("dve",
                  lambda e, pr=pr, r=r, dout=dout: e.scalar_tensor_tensor(
                      ypre_bf[:, 512 * dout:512 * (dout + 1)], PB[pr][:],
                      bproj_sb[:, dout:dout + 1], xr_buf[r][:],
                      op0=ALU.add, op1=ALU.add),
                  reads=(b_PB[pr], b_xr[r], b_tabs), writes=(b_ypre,), tag="ypre")
            mm(PB[2][64:65, :], onesc_sb[:], ypre_bf[:, 512 * dout:512 * (dout + 1)],
               start=(dout == 0), stop=(dout == NC - 1),
               reads=(b_ypre, b_tabs), writes=(b_PB[2],), tag="mean1")
            em.pull(7)
        em.op("act",
              lambda e: e.activation(out=nmean_sb[0:1, :], in_=PB[2][64:65, :],
                                     func=AF.Copy, scale=-1.0 / 2048.0),
              reads=(b_PB[2],), writes=(b_nmean,), tag="nmean")
        em.drain()
        mm(PB[4][:], onesr_sb[:], nmean_sb[0:1, :], start=True, stop=True,
           reads=(b_nmean, b_tabs), writes=(b_PB[4],), tag="meanB")
        for dc in range(NC):
            em.op("dve",
                  lambda e, dc=dc: e.tensor_tensor(
                      ypre_bf[:, 512 * dc:512 * (dc + 1)],
                      ypre_bf[:, 512 * dc:512 * (dc + 1)], PB[4][:], op=ALU.add),
                  reads=(b_PB[4], b_ypre), writes=(b_ypre,), tag="msub")

    def push_up(b, g):
        first = (b == 0 and g == 0)

        def up_load(hc):
            r = hc % 2
            if first:
                s2 = upT[:, 128 * hc:128 * (hc + 1)].rearrange("(dc p) j -> p dc j", p=128)
                dma("pool", cvt_f[r][:].rearrange("p (dc j) -> p dc j", j=128), s2,
                    reads=(), writes=(b_cvtf[r],), tag="up_cvld")
                em.op("act",
                      lambda e, r=r: e.activation(out=wq_buf[r][:], in_=cvt_f[r][:], func=AF.Sign),
                      reads=(b_cvtf[r],), writes=(b_wq[r],), tag="sign_up")
                dma("pool", up_scr[128 * hc:128 * (hc + 1), :], wq_buf[r][:],
                    reads=(b_wq[r],), writes=(b_upscr,), tag="up_cvst")
            else:
                dma("sync", wq_buf[r][:], up_scr[128 * hc:128 * (hc + 1), :],
                    reads=(b_upscr,), writes=(b_wq[r],), tag="up_ld")

        def up_half(hc, half):
            r = hc % 2
            pr = (4, 6, 7)[hc % 3]
            for dc in range(8 * half, 8 * half + 8):
                mm(PB[pr][:], wq_buf[r][:, 128 * dc:128 * (dc + 1)],
                   ypre_bf[:, 512 * dc:512 * (dc + 1)],
                   start=(dc == 0), stop=(dc == NC - 1),
                   reads=(b_wq[r], b_ypre), writes=(b_PB[pr],), tag="up")
            if half == 1:
                if hc % 2 == 0:
                    em.op("act",
                          lambda e, pr=pr, hc=hc: e.activation(
                              out=big_bf[:, 512 * hc:512 * (hc + 1)], in_=PB[pr][:],
                              func=AF.Relu, scale=us_sb[:, hc:hc + 1]),
                          reads=(b_PB[pr], b_tabs), writes=(b_big,), tag="relu")
                else:
                    em.op("dve",
                          lambda e, pr=pr, hc=hc: e.tensor_scalar(
                              big_bf[:, 512 * hc:512 * (hc + 1)], PB[pr][:],
                              us_sb[:, hc:hc + 1], 0.0, op0=ALU.mult, op1=ALU.max),
                          reads=(b_PB[pr], b_tabs), writes=(b_big,), tag="relu")
                if hc + 2 < HC:
                    up_load(hc + 2)

        up_load(0)
        up_load(1)
        for hc in range(HC):
            em.push(lambda hc=hc: up_half(hc, 0))
            em.push(lambda hc=hc: up_half(hc, 1))

    def push_down_ln2(b, g):
        first = (b == 0 and g == 0)

        def dn_load(dout, half):
            r = half
            if first:
                s2 = dnT[4096 * half:4096 * (half + 1),
                         128 * dout:128 * (dout + 1)].rearrange("(hc p) j -> p hc j", p=128)
                dma("pool", dnstage[:].rearrange("p (hc j) -> p hc j", j=128), s2,
                    reads=(), writes=(b_dnstage,), tag="dn_cvld")
                em.op("act",
                      lambda e, r=r: e.activation(out=dn_buf[r][:], in_=dnstage[:], func=AF.Sign),
                      reads=(b_dnstage,), writes=(b_dn[r],), tag="sign_dn")
                dma("pool", dn_scr[dout, half], dn_buf[r][:],
                    reads=(b_dn[r],), writes=(b_dnscr,), tag="dn_cvst")
            else:
                dma("sync", dn_buf[r][:], dn_scr[dout, half],
                    reads=(b_dnscr,), writes=(b_dn[r],), tag="dn_ld")

        def dn_quarter(dout, q):
            pr = (0, 1, 3, 5)[dout % 4]
            r = q // 2
            if q == 2 and dout + 1 < NC:
                dn_load(dout + 1, 0)
            for hh in range(16):
                hc = 16 * q + hh
                mm(PB[pr][:], dn_buf[r][:, 128 * (hc - 32 * r):128 * (hc - 32 * r + 1)],
                   big_bf[:, 512 * hc:512 * (hc + 1)],
                   start=(hc == 0), stop=(hc == HC - 1),
                   reads=(b_dn[r], b_big), writes=(b_PB[pr],), tag="dn")

        def dn_tail(dout):
            if dout + 1 < NC:
                dn_load(dout + 1, 1)
            pr = (0, 1, 3, 5)[dout % 4]
            em.op("dve",
                  lambda e, pr=pr, dout=dout: e.scalar_tensor_tensor(
                      v2_sb[:, 512 * dout:512 * (dout + 1)], PB[pr][:],
                      csc_sb[:, dout:dout + 1], ypre_bf[:, 512 * dout:512 * (dout + 1)],
                      op0=ALU.mult, op1=ALU.add),
                  reads=(b_PB[pr], b_ypre, b_csc), writes=(b_v2,), tag="v2")
            sq = dout % 2
            em.op("act",
                  lambda e, sq=sq, dout=dout: e.activation(
                      out=v2sq_buf[sq][:], in_=v2_sb[:, 512 * dout:512 * (dout + 1)],
                      func=AF.Square),
                  reads=(b_v2,), writes=(b_v2sq,), tag="v2sq")
            mm(PB[2][0:1, :], onescf_sb[:], v2_sb[:, 512 * dout:512 * (dout + 1)],
               start=(dout == 0), stop=(dout == NC - 1),
               reads=(b_v2, b_tabs), writes=(b_PB[2],), tag="m2mm")
            mm(PB[2][32:33, :], onesc_sb[:], v2sq_buf[sq][:],
               start=(dout == 0), stop=(dout == NC - 1),
               reads=(b_v2sq, b_tabs), writes=(b_PB[2],), tag="sq2mm")

        def ln2_rows():
            em.op("act", lambda e: e.activation(out=rows_sb[0:1, 0:512], in_=PB[2][0:1, :],
                                                func=AF.Copy, scale=1.0 / 2048.0),
                  reads=(b_PB[2],), writes=(b_rows,), tag="m2")
            em.op("act", lambda e: e.activation(out=rows_sb[0:1, _rowA:_rowA + 512],
                                                in_=rows_sb[0:1, 0:512], func=AF.Square),
                  reads=(b_rows,), writes=(b_rows,), tag="m2sq")
            em.op("dve",
                  lambda e: e.scalar_tensor_tensor(rows_sb[0:1, _rowB:_rowB + 512],
                                                   PB[2][32:33, :],
                                                   1.0 / 2048.0, rows_sb[0:1, _rowA:_rowA + 512],
                                                   op0=ALU.mult, op1=ALU.subtract),
                  reads=(b_PB[2], b_rows), writes=(b_rows,), tag="var")
            em.op("dve", lambda e: e.tensor_single_scalar(rows_sb[0:1, _rowB:_rowB + 512],
                                                          rows_sb[0:1, _rowB:_rowB + 512],
                                                          1.0e-5, ALU.add),
                  reads=(b_rows,), writes=(b_rows,), tag="epsadd")
            em.op("act", lambda e: e.activation(out=rows_sb[0:1, _rowA:_rowA + 512],
                                                in_=rows_sb[0:1, _rowB:_rowB + 512],
                                                func=AF.Sqrt),
                  reads=(b_rows,), writes=(b_rows,), tag="sd")
            em.op("dve", lambda e: e.reciprocal(rows_sb[0:1, _rowB:_rowB + 512],
                                                rows_sb[0:1, _rowA:_rowA + 512]),
                  reads=(b_rows,), writes=(b_rows,), tag="rstd")
            em.op("dve", lambda e: e.tensor_mul(rows_sb[0:1, _rowA:_rowA + 512],
                                                rows_sb[0:1, 0:512],
                                                rows_sb[0:1, _rowB:_rowB + 512]),
                  reads=(b_rows,), writes=(b_rows,), tag="nmr0")
            mm(PB[4][:], onesrf_sb[:], rows_sb[0:1, _rowB:_rowB + 512], start=True, stop=True,
               reads=(b_rows, b_tabs), writes=(b_PB[4],), tag="rstdB")

        def ap1(dout):
            em.op("dve",
                  lambda e, dout=dout: e.tensor_tensor(
                      v2_sb[:, 512 * dout:512 * (dout + 1)],
                      v2_sb[:, 512 * dout:512 * (dout + 1)], PB[4][:], op=ALU.mult),
                  reads=(b_PB[4], b_v2), writes=(b_v2,), tag="ap1")

        def nmrB():
            mm(PB[4][:], onesrf_sb[:], rows_sb[0:1, _rowA:_rowA + 512], start=True, stop=True,
               reads=(b_rows, b_tabs), writes=(b_PB[4],), tag="nmrB")

        def ap2(dout):
            em.op("dve",
                  lambda e, dout=dout: e.tensor_tensor(
                      v2_sb[:, 512 * dout:512 * (dout + 1)],
                      v2_sb[:, 512 * dout:512 * (dout + 1)],
                      PB[4][:], op=ALU.subtract),
                  reads=(b_PB[4], b_v2), writes=(b_v2,), tag="ap2")

        def y_store():
            dst = y_fm[b, :, 512 * g:512 * (g + 1)].rearrange("(dc p) t -> p dc t", p=128)
            src_ = v2_sb[:].rearrange("p (dc t) -> p dc t", t=512)
            dma("act", dst, src_, reads=(b_v2,), writes=(b_out,), tag="y_st")

        em.push(lambda: em.fence(b_v2, [b_k2, b_kn2] + b_cvtf))
        if b == 0 and g == 0:
            em.push(lambda: em.fence(b_dnstage, b_pu))
            em.push(lambda: dn_load(0, 0))
            em.push(lambda: dn_load(0, 1))
        else:
            dn_load(0, 0)
            dn_load(0, 1)
        for dout in range(NC):
            for q in range(4):
                em.push(lambda dout=dout, q=q: dn_quarter(dout, q))
            em.push(lambda dout=dout: dn_tail(dout))
        em.push(ln2_rows)
        for dout in range(NC):
            em.push(lambda dout=dout: ap1(dout))
        em.push(nmrB)
        for dout in range(NC):
            em.push(lambda dout=dout: ap2(dout))
        em.push(y_store)

    for b in range(B_loc):
        p1_with_p2g0(b)
        for g in range(G):
            p3(b, g)
            push_up(b, g)
            if g + 1 < G:
                for c in range(16):
                    em.fence(b_pu[c], [b_dnstage])
                for h in range(NH):
                    p2head(b, g + 1, h)
                em.drain()
            else:
                em.drain()
            push_down_ln2(b, g)
    em.drain()

    with nc.Block() as block:
        em.resolve_and_emit(nc, block)
    return nc


def _host_stage(inputs, B_loc, S):
    x = np.asarray(inputs["L0_attn_x"], dtype=np.float32)
    x_fm = np.ascontiguousarray(x.transpose(0, 2, 1))
    wqkv = np.ascontiguousarray(np.asarray(inputs["L0_attn_W_qkv"], np.float32))
    wproj = np.ascontiguousarray(np.asarray(inputs["L0_attn_W_proj"], np.float32))
    upT = np.ascontiguousarray(np.asarray(inputs["L0_ffn_up_weight"], np.float32).T)
    dnT = np.ascontiguousarray(np.asarray(inputs["L0_ffn_down_weight"], np.float32).T)
    bqkv = np.asarray(inputs["L0_attn_b_qkv"], np.float32)
    bqkv_t = np.ascontiguousarray(bqkv.reshape(18, 128).T)
    bv_row = np.ascontiguousarray(bqkv[H + HD:].reshape(1, 128))
    bproj_t = np.ascontiguousarray(
        np.asarray(inputs["L0_attn_b_proj"], np.float32).reshape(NC, 128).T)
    us_t = np.ascontiguousarray(
        np.asarray(inputs["L0_ffn_up_scale"], np.float32).reshape(HC, 128).T)
    ds_t = np.ascontiguousarray(
        np.asarray(inputs["L0_ffn_down_scale"], np.float32).reshape(NC, 128).T)
    os_col = np.ascontiguousarray(
        np.broadcast_to(np.asarray(inputs["L0_ffn_output_scale"], np.float32).reshape(1, 1),
                        (128, 1)))
    BF = ml_dtypes.bfloat16
    utri = np.triu(np.ones((128, 128), np.float32), 1).astype(BF)
    negi = (NEG_BIG * np.eye(128, dtype=np.float32)).astype(BF)
    shared = dict(wqkv=wqkv, wproj=wproj, upT=upT, dnT=dnT, bqkv_t=bqkv_t,
                  bv_row=bv_row, bproj_t=bproj_t, us_t=us_t, ds_t=ds_t, os_col=os_col,
                  utri_in=utri, negi_in=negi,
                  onesc_in=np.ones((128, 1), BF),
                  onescf_in=np.ones((128, 1), np.float32),
                  onesr_in=np.ones((1, 128), BF),
                  onesrf_in=np.ones((1, 128), np.float32))
    n_cores = x.shape[0] // B_loc
    in_maps = []
    for i in range(n_cores):
        m = dict(shared)
        m["x_fm"] = np.ascontiguousarray(x_fm[i * B_loc:(i + 1) * B_loc, :, :S])
        in_maps.append(m)
    return in_maps


def kernel(**inputs):
    B_loc, S = 2, 2048
    in_maps = _host_stage(inputs, B_loc, S)
    nc = build_program(B_loc, S)
    res = run_bass_kernel_spmd(nc, in_maps, core_ids=list(range(8)))
    shards = [np.asarray(r["y_fm"]) for r in res.results]
    y_fm = np.concatenate(shards, axis=0)
    return np.ascontiguousarray(y_fm.transpose(0, 2, 1)).astype(np.float32)


# version 25
# speedup vs baseline: 1.0423x; 1.0423x over previous
import numpy as np
import ml_dtypes

import concourse.bass as bass
import concourse.mybir as mybir
from concourse.bass_utils import run_bass_kernel_spmd

F32 = mybir.dt.float32
BF16 = mybir.dt.bfloat16
AF = mybir.ActivationFunctionType
ALU = mybir.AluOpType
AX = mybir.AxisListType

H = 2048
NH = 16
HD = 128
DH = 8192
NC = H // 128
HC = DH // 128
SCALE = float(HD) ** -0.5
NEG_BIG = -1.0e30

ENGINES = ("pe", "act", "dve", "sync", "pool")
DMA_ENG = {"sync", "pool", "act"}


class Buf:
    __slots__ = ("name", "writer", "readers")

    def __init__(self, name):
        self.name = name
        self.writer = None
        self.readers = []


class Emitter:
    RING = 12

    def __init__(self):
        self.ops = {e: [] for e in ENGINES}
        self.queue = []
        self.qhead = 0

    def op(self, eng, fn, reads=(), writes=(), tag="", is_dma=False):
        idx = len(self.ops[eng])
        deps = set()
        for b in reads:
            if b.writer is not None:
                deps.add(b.writer)
        for b in writes:
            if b.writer is not None:
                deps.add(b.writer)
            for r in b.readers:
                deps.add(r)
        me = (eng, idx)
        for b in writes:
            b.writer = me
            b.readers = []
        for b in reads:
            b.readers.append(me)
        deps.discard(me)
        if eng == "pe":
            deps = {d for d in deps if d[0] != "pe"}
        self.ops[eng].append([fn, deps, False, tag, is_dma])

    def fence(self, target_buf, source_bufs):
        for s in source_bufs:
            if s.writer is not None:
                target_buf.readers.append(s.writer)
            target_buf.readers.extend(s.readers)

    def push(self, fn):
        self.queue.append(fn)

    def pull(self, n):
        while n > 0 and self.qhead < len(self.queue):
            self.queue[self.qhead]()
            self.qhead += 1
            n -= 1

    def drain(self):
        self.pull(len(self.queue) - self.qhead)

    def resolve_and_emit(self, nc, block):
        RING = self.RING
        assert self.qhead == len(self.queue), "filler queue not drained"
        for e in ENGINES:
            for rec in self.ops[e]:
                for (pe_, pi) in rec[1]:
                    self.ops[pe_][pi][2] = True
                if rec[4]:
                    rec[2] = True
        key, val, dmai = {}, {}, {}
        for e in ENGINES:
            d = 0
            c = 0
            for i, rec in enumerate(self.ops[e]):
                if rec[4]:
                    key[(e, i)] = (e, d % RING)
                    val[(e, i)] = 16 * (d // RING + 1)
                    dmai[(e, i)] = d
                    d += 1
                else:
                    if rec[2]:
                        c += 1
                    key[(e, i)] = e
                    val[(e, i)] = c
        semkeys = list(ENGINES)
        semkeys += [(e, s) for e in DMA_ENG for s in range(RING)]
        sems = {k: nc.alloc_semaphore(f"sem_{k}") for k in semkeys}

        def emit_stream(eng_obj, e):
            hw = {}
            n_ops = len(self.ops[e])
            for i, rec in enumerate(self.ops[e]):
                fn, deps, need_inc, tag, is_dma = rec
                per_sem = {}
                for dep in deps:
                    k = key[dep]
                    v = val[dep]
                    if v > per_sem.get(k, 0):
                        per_sem[k] = v
                if is_dma:
                    d = dmai[(e, i)]
                    if d >= RING:
                        k = (e, d % RING)
                        v = 16 * (d // RING)
                        if v > per_sem.get(k, 0):
                            per_sem[k] = v
                for k, v in per_sem.items():
                    if v > hw.get(k, 0):
                        eng_obj.wait_ge(sems[k], v)
                        hw[k] = v
                inst = fn(eng_obj)
                if need_inc:
                    inst.then_inc(sems[key[(e, i)]], 16 if is_dma else 1)
            if e in DMA_ENG and n_ops:
                last = {}
                for i in range(n_ops):
                    k = key[(e, i)]
                    if isinstance(k, tuple):
                        last[k] = val[(e, i)]
                for k, v in sorted(last.items()):
                    eng_obj.wait_ge(sems[k], v)

        @block.tensor
        def _(t):
            emit_stream(t, "pe")

        @block.scalar
        def _(s):
            emit_stream(s, "act")

        @block.vector
        def _(v):
            emit_stream(v, "dve")

        @block.sync
        def _(s):
            emit_stream(s, "sync")

        @block.gpsimd
        def _(g):
            emit_stream(g, "pool")


def build_program(B_loc=2, S=2048):
    G = S // 512
    nc = bass.Bass("TRN2", target_bir_lowering=False)

    x_fm = nc.dram_tensor("x_fm", [B_loc, H, S], F32, kind="ExternalInput").ap()
    wqkv = nc.dram_tensor("wqkv", [H, H + 2 * HD], F32, kind="ExternalInput").ap()
    wproj = nc.dram_tensor("wproj", [H, H], F32, kind="ExternalInput").ap()
    upT = nc.dram_tensor("upT", [H, DH], F32, kind="ExternalInput").ap()
    dnT = nc.dram_tensor("dnT", [DH, H], F32, kind="ExternalInput").ap()
    bqkv_t = nc.dram_tensor("bqkv_t", [128, 18], F32, kind="ExternalInput").ap()
    bv_row = nc.dram_tensor("bv_row", [1, 128], F32, kind="ExternalInput").ap()
    bproj_t = nc.dram_tensor("bproj_t", [128, NC], F32, kind="ExternalInput").ap()
    us_t = nc.dram_tensor("us_t", [128, HC], F32, kind="ExternalInput").ap()
    ds_t = nc.dram_tensor("ds_t", [128, NC], F32, kind="ExternalInput").ap()
    os_col = nc.dram_tensor("os_col", [128, 1], F32, kind="ExternalInput").ap()
    utri_in = nc.dram_tensor("utri_in", [128, 128], BF16, kind="ExternalInput").ap()
    negi_in = nc.dram_tensor("negi_in", [128, 128], BF16, kind="ExternalInput").ap()
    onesc_in = nc.dram_tensor("onesc_in", [128, 1], BF16, kind="ExternalInput").ap()
    onescf_in = nc.dram_tensor("onescf_in", [128, 1], F32, kind="ExternalInput").ap()
    onesr_in = nc.dram_tensor("onesr_in", [1, 128], BF16, kind="ExternalInput").ap()
    onesrf_in = nc.dram_tensor("onesrf_in", [1, 128], F32, kind="ExternalInput").ap()

    y_fm = nc.dram_tensor("y_fm", [B_loc, H, S], F32, kind="ExternalOutput").ap()

    q_dram = nc.dram_tensor("q_dram", [B_loc, NH, G, 128, 512], BF16, kind="Internal").ap()
    up_scr = nc.dram_tensor("up_scr", [DH, H], BF16, kind="Internal").ap()
    dn_scr = nc.dram_tensor("dn_scr", [NC, 2, 128, 32 * 128], BF16, kind="Internal").ap()
    wp_scr = nc.dram_tensor("wp_scr", [NC, 128, H], BF16, kind="Internal").ap()

    A = nc.alloc_sbuf_tensor
    big_bf = A("big_bf", [128, 16 * 2048], BF16)
    qt_buf = [A(f"qt{i}", [128, 512], BF16) for i in range(2)]
    q_stage = [A(f"qst{i}", [128, 512], BF16) for i in range(2)]
    k_sb = A("k_sb", [128, S], BF16)
    vtok_sb = A("vtok_sb", [128, S], BF16)
    attn_sb = A("attn_sb", [128, NC * 512], BF16)
    ypre_bf = A("ypre_bf", [128, NC * 512], BF16)
    v2_sb = A("v2_sb", [128, NC * 512], F32)
    probsu_sb = A("probsu", [128, 16 * 512], BF16)
    tmp_f = [A(f"tmpf{i}", [128, 512], F32) for i in range(2)]
    cB_sb = [A(f"cB{i}", [128, 512], F32) for i in range(2)]
    rB_sb = A("rB_sb", [128, 512], F32)
    crow_sb = A("crow", [1, 512], BF16)
    q2t_sb = A("q2t", [128, 512], BF16)
    wq_buf = [A(f"wq{i}", [128, 16 * 128], BF16) for i in range(2)]
    dn_buf = [A(f"dn{i}", [128, 32 * 128], BF16) for i in range(2)]
    xr_buf = [A(f"xr{i}", [128, 512], F32) for i in range(2)]
    kmax2_sb = A("kmax2", [1, 1], F32)
    nmean_sb = A("nmean", [1, 512], BF16)
    rows_sb = A("rows", [1, 3 * 512], F32)
    _rowA = 512
    _rowB = 1024
    bqkv_sb = A("bqkv_sb", [128, 18], F32)
    bqs_sb = A("bqs_sb", [128, 16], F32)
    bvr_sb = A("bvr", [1, 128], BF16)
    bproj_sb = A("bproj_sb", [128, NC], F32)
    us_sb = A("us_sb", [128, HC], F32)
    ds_sb = A("ds_sb", [128, NC], F32)
    os_sb = A("os_sb", [128, 1], F32)
    csc_sb = A("csc_sb", [128, NC], F32)
    utri_sb = A("utri", [128, 128], BF16)
    negi_sb = A("negi", [128, 128], BF16)
    onesc_sb = A("onesc", [128, 1], BF16)
    onescf_sb = A("onescf", [128, 1], F32)
    onesr_sb = A("onesr", [1, 128], BF16)
    onesrf_sb = A("onesrf", [1, 128], F32)

    def addr_of(h):
        return nc.lookup_mloc(h).addr

    v2_addr = addr_of(v2_sb)
    k2_sb = nc.alloc_sbuf_tensor_at("k2_sb", [128, 2048], BF16, offset=v2_addr + 16384)
    kn2_sb = nc.alloc_sbuf_tensor_at("kn2_sb", [1, 2048], BF16, offset=v2_addr + 20480)
    yp_addr = addr_of(ypre_bf)
    cvt_f = [nc.alloc_sbuf_tensor_at(f"cvtf{i}", [128, 2048], F32, offset=v2_addr + i * 8192)
             for i in range(2)]
    pu_addr = addr_of(probsu_sb)
    dnstage = nc.alloc_sbuf_tensor_at("dnstage", [128, 4096], F32, offset=pu_addr)
    cvt_b = [nc.alloc_sbuf_tensor_at(f"cvtb{i}", [128, 2048], BF16, offset=yp_addr + i * 4096)
             for i in range(2)]
    wp_buf = [nc.alloc_sbuf_tensor_at(f"wp{i}", [128, 16 * 128], BF16, offset=addr_of(wq_buf[i]))
              for i in range(2)]
    v2sq_buf = [nc.alloc_sbuf_tensor_at(f"v2sq{i}", [128, 512], BF16,
                                        offset=addr_of(tmp_f[0]) + i * 1024)
                for i in range(2)]
    rem = nc.sbuf_bytes_remaining
    assert rem >= 0, rem

    PB = [nc.alloc_psum_tensor(f"PB{i}", [128, 512], F32) for i in range(8)]

    em = Emitter()
    B_ = Buf
    b_bigc = [B_(f"big{i}") for i in range(NC)]
    b_cvtf = [B_("cvtf0"), B_("cvtf1")]
    b_cvtb = [B_("cvtb0"), B_("cvtb1")]
    b_qt = [B_("qt0"), B_("qt1")]
    b_qst = [B_("qst0"), B_("qst1")]
    b_k = B_("k")
    b_vtok = B_("vtok")
    b_attn = B_("attn")
    b_ypre = B_("ypre")
    b_v2 = B_("v2")
    b_pu = [B_(f"pu{c}") for c in range(16)]
    b_k2 = B_("k2")
    b_kn2 = B_("kn2")
    b_tmp = [B_("tmp0"), B_("tmp1")]
    b_cB = [B_("cB0"), B_("cB1")]
    b_rB = B_("rB")
    b_q2t = B_("q2t")
    b_wq = [B_("wqup0"), B_("wqup1")]
    b_dn = [B_("wpdn0"), B_("wpdn1")]
    b_xr = [B_("xr0"), B_("xr1")]
    b_v2sq = b_tmp[0]
    b_dnstage = B_("dnstage")
    b_kmax = B_("kmax")
    b_nmean = B_("nmean")
    b_rows = B_("rows")
    b_tabs = B_("tabs")
    b_csc = B_("csc")
    b_bqs = B_("bqs")
    b_PB = [B_(f"PB{i}") for i in range(8)]
    b_qdram = B_("qdram")
    b_upscr = B_("upscr")
    b_dnscr = B_("dnscr")
    b_wpscr = B_("wpscr")
    b_out = B_("out")

    def dma(queue, dst, src, reads, writes, tag=""):
        em.op(queue, lambda e, d=dst, s=src: e.dma_start(out=d, in_=s),
              reads=reads, writes=writes, tag=tag, is_dma=True)

    def mm(out_ap, lhs_ap, rhs_ap, start, stop, reads, writes, tag=""):
        em.op("pe",
              lambda e, o=out_ap, l=lhs_ap, r=rhs_ap, st=start, sp=stop:
                  e.matmul(o, l, r, start=st, stop=sp),
              reads=reads, writes=writes, tag=tag)

    for dst, src in ((bqkv_sb, bqkv_t), (bproj_sb, bproj_t), (us_sb, us_t),
                     (ds_sb, ds_t), (os_sb, os_col), (utri_sb, utri_in),
                     (negi_sb, negi_in), (onesc_sb, onesc_in), (onescf_sb, onescf_in),
                     (onesr_sb, onesr_in), (onesrf_sb, onesrf_in)):
        dma("sync", dst[:], src, reads=(), writes=(b_tabs,), tag="consts")
    dma("pool", bvr_sb[:], bv_row, reads=(), writes=(b_tabs,), tag="bvr")
    em.op("dve", lambda e: e.tensor_single_scalar(csc_sb[:], ds_sb[:], os_sb[:, 0:1], ALU.mult),
          reads=(b_tabs,), writes=(b_csc,), tag="csc")
    em.op("dve", lambda e: e.tensor_single_scalar(bqs_sb[:], bqkv_sb[:, 0:16], float(SCALE), ALU.mult),
          reads=(b_tabs,), writes=(b_bqs,), tag="bqs")

    def emit_x_loads(b):
        for dc in range(NC):
            rd = (b_dn[0], b_dn[1]) if (b > 0 and dc == 0) else ()
            dma("pool", big_bf[:, S * dc:S * (dc + 1)],
                x_fm[b, 128 * dc:128 * (dc + 1), 0:S],
                reads=rd, writes=(b_bigc[dc],), tag=f"xbf{b}_{dc}")

    def emit_wqkv_load(b, j):
        r = j % 2
        src = wqkv[:, 128 * j:128 * (j + 1)].rearrange("(dc p) j -> p dc j", p=128)
        dma("pool", wq_buf[r][:].rearrange("p (dc j) -> p dc j", j=128), src,
            reads=(), writes=(b_wq[r],), tag=f"wqkv{b}_{j}")

    emit_x_loads(0)
    emit_wqkv_load(0, 16)
    emit_wqkv_load(0, 17)

    def qkv_head_group(b, j):
        r = j % 2
        for tb in range(G):
            pr = 6 + (j * G + tb) % 2
            for dc in range(NC):
                mm(PB[pr][:], wq_buf[r][:, 128 * dc:128 * (dc + 1)],
                   big_bf[:, S * dc + 512 * tb:S * dc + 512 * (tb + 1)],
                   start=(dc == 0), stop=(dc == NC - 1),
                   reads=(b_wq[r], b_bigc[dc]), writes=(b_PB[pr],), tag=f"qkv{j}")
            if j < 16:
                sr = (j * G + tb) % 2
                em.op("dve",
                      lambda e, pr=pr, sr=sr, j=j: e.tensor_scalar(
                          q_stage[sr][:], PB[pr][:], float(SCALE), bqs_sb[:, j:j + 1],
                          op0=ALU.mult, op1=ALU.add),
                      reads=(b_PB[pr], b_bqs), writes=(b_qst[sr],), tag="q_evac")
                dma("act", q_dram[b, j, tb], q_stage[sr][:],
                    reads=(b_qst[sr],), writes=(b_qdram,), tag="q_st")
            else:
                em.op("dve",
                      lambda e, pr=pr, tb=tb: e.tensor_scalar(
                          k_sb[:, 512 * tb:512 * (tb + 1)], PB[pr][:],
                          bqkv_sb[:, 16:17], None, op0=ALU.add),
                      reads=(b_PB[pr], b_tabs), writes=(b_k,), tag="k_evac")

    def vtok_group(b):
        r = 17 % 2
        for c in range(S // 128):
            for dc in range(NC):
                mm(PB[2][:, 0:128],
                   big_bf[:, S * dc + 128 * c:S * dc + 128 * (c + 1)],
                   wq_buf[r][:, 128 * dc:128 * (dc + 1)],
                   start=(dc == 0), stop=False,
                   reads=(b_bigc[dc], b_wq[r]), writes=(b_PB[2],), tag="vtok")
            mm(PB[2][:, 0:128], onesr_sb[:], bvr_sb[0:1, :],
               start=False, stop=True, reads=(b_tabs,), writes=(b_PB[2],), tag="vtok_b")
            em.op("act",
                  lambda e, c=c: e.activation(
                      out=vtok_sb[:, 128 * c:128 * (c + 1)], in_=PB[2][:, 0:128],
                      func=AF.Copy),
                  reads=(b_PB[2],), writes=(b_vtok,), tag="vtok_evac")

    def k_stats(b):
        em.fence(b_k2, [b_v2])
        em.fence(b_kn2, [b_v2])
        em.op("act", lambda e: e.activation(out=k2_sb[:, 0:S], in_=k_sb[:, 0:S], func=AF.Square),
              reads=(b_k,), writes=(b_k2,), tag="k2")
        for t4 in range(G):
            mm(PB[3][0:1, :], onesc_sb[:], k2_sb[:, 512 * t4:512 * (t4 + 1)],
               start=True, stop=True, reads=(b_k2, b_tabs), writes=(b_PB[3],), tag="kn2mm")
            em.op("act",
                  lambda e, t4=t4: e.activation(
                      out=kn2_sb[0:1, 512 * t4:512 * (t4 + 1)], in_=PB[3][0:1, :], func=AF.Copy),
                  reads=(b_PB[3],), writes=(b_kn2,), tag="kn2_evac")
        em.op("dve", lambda e: e.reduce_max(kmax2_sb[0:1, 0:1], kn2_sb[0:1, 0:S], axis=AX.X),
              reads=(b_kn2,), writes=(b_kmax,), tag="kmax2")

    def p2head(b, g, h, light=False, budget=8):
        C = 4 * g + 4
        bud = [1 if light else budget]

        def pull_b(n):
            n = min(n, bud[0])
            if n > 0:
                em.pull(n)
                bud[0] -= n
        qr = h % 2
        cr = h % 2
        if h == 0:
            dma("sync", qt_buf[0][:], q_dram[b, 0, g],
                reads=(b_qdram,), writes=(b_qt[0],), tag="q_ld")
        if h + 1 < NH:
            dma("sync", qt_buf[(h + 1) % 2][:], q_dram[b, h + 1, g],
                reads=(b_qdram,), writes=(b_qt[(h + 1) % 2],), tag="q_ld")
        em.op("act", lambda e, qr=qr: e.activation(out=q2t_sb[:], in_=qt_buf[qr][:], func=AF.Square),
              reads=(b_qt[qr],), writes=(b_q2t,), tag="q2")
        mm(PB[3][32:33, :], onesc_sb[:], q2t_sb[:], start=True, stop=True,
           reads=(b_q2t, b_tabs), writes=(b_PB[3],), tag="q2sum")
        em.op("act",
              lambda e: e.activation(out=crow_sb[0:1, :], in_=PB[3][32:33, :],
                                     func=AF.Sqrt, scale=kmax2_sb[0:1, 0:1]),
              reads=(b_PB[3], b_kmax), writes=(b_rows,), tag="crow")
        mm(PB[3][:], onesr_sb[:], crow_sb[0:1, :], start=True, stop=True,
           reads=(b_rows, b_tabs), writes=(b_PB[3],), tag="cB_mm")
        em.op("dve", lambda e, cr=cr: e.tensor_copy(out=cB_sb[cr][:], in_=PB[3][:]),
              reads=(b_PB[3],), writes=(b_cB[cr],), tag="cB_cp")
        pull_b(2)

        def pv_s(c):
            col0 = 128 * (c - 4 * g) if c >= 4 * g else 0
            mm(PB[2][:, col0:512], vtok_sb[:, 128 * c:128 * (c + 1)],
               probsu_sb[:, 512 * c + col0:512 * (c + 1)],
               start=(c == 0), stop=(c == C - 1),
               reads=(b_vtok, b_pu[c]), writes=(b_PB[2],), tag="pv")
            mm(PB[3][0:1, col0:512], onesc_sb[:],
               probsu_sb[:, 512 * c + col0:512 * (c + 1)],
               start=(c == 0), stop=(c == C - 1),
               reads=(b_pu[c], b_tabs), writes=(b_PB[3],), tag="srow")

        LAG = 3
        for c in range(C):
            col0 = 128 * (c - 4 * g) if c >= 4 * g else 0
            NP = 512 - col0
            pr = (0, 1, 5)[c % 3]
            diag = c >= 4 * g
            mm(PB[pr][:, 0:NP], k_sb[:, 128 * c:128 * (c + 1)],
               qt_buf[qr][:, col0:512],
               start=True, stop=(not diag),
               reads=(b_k, b_qt[qr]), writes=(b_PB[pr],), tag="qk")
            if diag:
                mm(PB[pr][:, 0:128], utri_sb[:], negi_sb[:],
                   start=False, stop=True, reads=(b_tabs,), writes=(b_PB[pr],), tag="mask")
            tr = c % 2
            em.op("dve",
                  lambda e, pr=pr, tr=tr, NP=NP, col0=col0, cr=cr: e.tensor_tensor(
                      tmp_f[tr][:, 0:NP], PB[pr][:, 0:NP], cB_sb[cr][:, col0:512],
                      op=ALU.subtract),
                  reads=(b_PB[pr], b_cB[cr]), writes=(b_tmp[tr],), tag="subc")
            em.op("act",
                  lambda e, tr=tr, NP=NP, c=c, col0=col0: e.activation(
                      out=probsu_sb[:, 512 * c + col0:512 * (c + 1)],
                      in_=tmp_f[tr][:, 0:NP], func=AF.Exp),
                  reads=(b_tmp[tr],), writes=(b_pu[c],), tag="exp")
            if c >= LAG:
                pv_s(c - LAG)
            if c % 2 == 1 and bud[0] > 2:
                em.pull(1)
                bud[0] -= 1
        for c in range(max(0, C - LAG), C):
            pv_s(c)
        em.op("dve", lambda e: e.reciprocal(rows_sb[0:1, _rowA:_rowA + 512], PB[3][0:1, :]),
              reads=(b_PB[3],), writes=(b_rows,), tag="recip")
        pull_b(2)
        mm(PB[3][:], onesrf_sb[:], rows_sb[0:1, _rowA:_rowA + 512], start=True, stop=True,
           reads=(b_rows, b_tabs), writes=(b_PB[3],), tag="rB_mm")
        em.op("dve", lambda e: e.tensor_copy(out=rB_sb[:], in_=PB[3][:]),
              reads=(b_PB[3],), writes=(b_rB,), tag="rB_cp")
        em.op("dve",
              lambda e, h=h: e.tensor_tensor(
                  attn_sb[:, 512 * h:512 * (h + 1)], PB[2][:], rB_sb[:], op=ALU.mult),
              reads=(b_PB[2], b_rB), writes=(b_attn,), tag="attn_evac")

    def p1_with_p2g0(b):
        em.drain()
        if b > 0:
            emit_x_loads(b)
            emit_wqkv_load(b, 16)
            emit_wqkv_load(b, 17)
        qkv_head_group(b, 16)
        k_stats(b)
        vtok_group(b)
        emit_wqkv_load(b, 0)
        emit_wqkv_load(b, 1)

        def qkv_closure(j):
            qkv_head_group(b, j)
            if j + 2 < 16:
                emit_wqkv_load(b, j + 2)

        for j in range(16):
            em.push(lambda j=j: qkv_closure(j))
        for h in range(NH):
            em.pull(2 if h == 0 else 1)
            p2head(b, 0, h, light=True)
        em.drain()

    def p3(b, g):
        dma("sync", xr_buf[0][:], x_fm[b, 0:128, 512 * g:512 * (g + 1)],
            reads=(), writes=(b_xr[0],), tag="x_ld")
        first = (b == 0 and g == 0)

        def wp_load(dout):
            r = dout % 2
            if first:
                s2 = wproj[:, 128 * dout:128 * (dout + 1)].rearrange("(dc p) j -> p dc j", p=128)
                dma("pool", wp_buf[r][:].rearrange("p (dc j) -> p dc j", j=128), s2,
                    reads=(), writes=(b_wq[r],), tag="wp_cvld")
                dma("pool", wp_scr[dout], wp_buf[r][:],
                    reads=(b_wq[r],), writes=(b_wpscr,), tag="wp_cvst")
            else:
                dma("sync", wp_buf[r][:], wp_scr[dout],
                    reads=(b_wpscr,), writes=(b_wq[r],), tag="wp_ld")

        wp_load(0)
        wp_load(1)
        for dout in range(NC):
            r = dout % 2
            if dout + 1 < NC:
                dma("sync", xr_buf[(dout + 1) % 2][:],
                    x_fm[b, 128 * (dout + 1):128 * (dout + 2), 512 * g:512 * (g + 1)],
                    reads=(), writes=(b_xr[(dout + 1) % 2],), tag="x_ld")
            pr = 6 + dout % 2
            for dc in range(NC):
                mm(PB[pr][:], wp_buf[r][:, 128 * dc:128 * (dc + 1)],
                   attn_sb[:, 512 * dc:512 * (dc + 1)],
                   start=(dc == 0), stop=(dc == NC - 1),
                   reads=(b_wq[r], b_attn), writes=(b_PB[pr],), tag="proj")
            if dout + 2 < NC:
                wp_load(dout + 2)
            em.op("dve",
                  lambda e, pr=pr, r=r, dout=dout: e.scalar_tensor_tensor(
                      ypre_bf[:, 512 * dout:512 * (dout + 1)], PB[pr][:],
                      bproj_sb[:, dout:dout + 1], xr_buf[r][:],
                      op0=ALU.add, op1=ALU.add),
                  reads=(b_PB[pr], b_xr[r], b_tabs), writes=(b_ypre,), tag="ypre")
            mm(PB[2][64:65, :], onesc_sb[:], ypre_bf[:, 512 * dout:512 * (dout + 1)],
               start=(dout == 0), stop=(dout == NC - 1),
               reads=(b_ypre, b_tabs), writes=(b_PB[2],), tag="mean1")
            em.pull(7)
        em.op("act",
              lambda e: e.activation(out=nmean_sb[0:1, :], in_=PB[2][64:65, :],
                                     func=AF.Copy, scale=-1.0 / 2048.0),
              reads=(b_PB[2],), writes=(b_nmean,), tag="nmean")
        em.drain()
        mm(PB[4][:], onesr_sb[:], nmean_sb[0:1, :], start=True, stop=True,
           reads=(b_nmean, b_tabs), writes=(b_PB[4],), tag="meanB")
        for dc in range(NC):
            em.op("dve",
                  lambda e, dc=dc: e.tensor_tensor(
                      ypre_bf[:, 512 * dc:512 * (dc + 1)],
                      ypre_bf[:, 512 * dc:512 * (dc + 1)], PB[4][:], op=ALU.add),
                  reads=(b_PB[4], b_ypre), writes=(b_ypre,), tag="msub")

    def push_up(b, g):
        first = (b == 0 and g == 0)

        def up_load(hc):
            r = hc % 2
            if first:
                s2 = upT[:, 128 * hc:128 * (hc + 1)].rearrange("(dc p) j -> p dc j", p=128)
                dma("pool", cvt_f[r][:].rearrange("p (dc j) -> p dc j", j=128), s2,
                    reads=(), writes=(b_cvtf[r],), tag="up_cvld")
                em.op("act",
                      lambda e, r=r: e.activation(out=wq_buf[r][:], in_=cvt_f[r][:], func=AF.Sign),
                      reads=(b_cvtf[r],), writes=(b_wq[r],), tag="sign_up")
                dma("pool", up_scr[128 * hc:128 * (hc + 1), :], wq_buf[r][:],
                    reads=(b_wq[r],), writes=(b_upscr,), tag="up_cvst")
            else:
                dma("sync", wq_buf[r][:], up_scr[128 * hc:128 * (hc + 1), :],
                    reads=(b_upscr,), writes=(b_wq[r],), tag="up_ld")

        def up_half(hc, half):
            r = hc % 2
            pr = (4, 6, 7)[hc % 3]
            for dc in range(8 * half, 8 * half + 8):
                mm(PB[pr][:], wq_buf[r][:, 128 * dc:128 * (dc + 1)],
                   ypre_bf[:, 512 * dc:512 * (dc + 1)],
                   start=(dc == 0), stop=(dc == NC - 1),
                   reads=(b_wq[r], b_ypre), writes=(b_PB[pr],), tag="up")
            if half == 1:
                if hc % 2 == 0:
                    em.op("act",
                          lambda e, pr=pr, hc=hc: e.activation(
                              out=big_bf[:, 512 * hc:512 * (hc + 1)], in_=PB[pr][:],
                              func=AF.Relu, scale=us_sb[:, hc:hc + 1]),
                          reads=(b_PB[pr], b_tabs), writes=(b_bigc[hc // 4],), tag="relu")
                else:
                    em.op("dve",
                          lambda e, pr=pr, hc=hc: e.tensor_scalar(
                              big_bf[:, 512 * hc:512 * (hc + 1)], PB[pr][:],
                              us_sb[:, hc:hc + 1], 0.0, op0=ALU.mult, op1=ALU.max),
                          reads=(b_PB[pr], b_tabs), writes=(b_bigc[hc // 4],), tag="relu")
                if hc + 2 < HC:
                    up_load(hc + 2)

        up_load(0)
        up_load(1)
        for hc in range(HC):
            em.push(lambda hc=hc: up_half(hc, 0))
            em.push(lambda hc=hc: up_half(hc, 1))

    def push_down_ln2(b, g):
        first = (b == 0 and g == 0)

        def dn_load(dout, half):
            r = half
            if first:
                s2 = dnT[4096 * half:4096 * (half + 1),
                         128 * dout:128 * (dout + 1)].rearrange("(hc p) j -> p hc j", p=128)
                dma("pool", dnstage[:].rearrange("p (hc j) -> p hc j", j=128), s2,
                    reads=(), writes=(b_dnstage,), tag="dn_cvld")
                em.op("act",
                      lambda e, r=r: e.activation(out=dn_buf[r][:], in_=dnstage[:], func=AF.Sign),
                      reads=(b_dnstage,), writes=(b_dn[r],), tag="sign_dn")
                dma("pool", dn_scr[dout, half], dn_buf[r][:],
                    reads=(b_dn[r],), writes=(b_dnscr,), tag="dn_cvst")
            else:
                dma("sync", dn_buf[r][:], dn_scr[dout, half],
                    reads=(b_dnscr,), writes=(b_dn[r],), tag="dn_ld")

        def dn_quarter(dout, q):
            pr = (0, 1, 3, 5)[dout % 4]
            r = q // 2
            if q == 2 and dout + 1 < NC:
                dn_load(dout + 1, 0)
            for hh in range(16):
                hc = 16 * q + hh
                mm(PB[pr][:], dn_buf[r][:, 128 * (hc - 32 * r):128 * (hc - 32 * r + 1)],
                   big_bf[:, 512 * hc:512 * (hc + 1)],
                   start=(hc == 0), stop=(hc == HC - 1),
                   reads=(b_dn[r], b_bigc[hc // 4]), writes=(b_PB[pr],), tag="dn")

        def dn_tail(dout):
            if dout + 1 < NC:
                dn_load(dout + 1, 1)
            pr = (0, 1, 3, 5)[dout % 4]
            em.op("dve",
                  lambda e, pr=pr, dout=dout: e.scalar_tensor_tensor(
                      v2_sb[:, 512 * dout:512 * (dout + 1)], PB[pr][:],
                      csc_sb[:, dout:dout + 1], ypre_bf[:, 512 * dout:512 * (dout + 1)],
                      op0=ALU.mult, op1=ALU.add),
                  reads=(b_PB[pr], b_ypre, b_csc), writes=(b_v2,), tag="v2")
            sq = dout % 2
            em.op("act",
                  lambda e, sq=sq, dout=dout: e.activation(
                      out=v2sq_buf[sq][:], in_=v2_sb[:, 512 * dout:512 * (dout + 1)],
                      func=AF.Square),
                  reads=(b_v2,), writes=(b_v2sq,), tag="v2sq")
            mm(PB[2][0:1, :], onescf_sb[:], v2_sb[:, 512 * dout:512 * (dout + 1)],
               start=(dout == 0), stop=(dout == NC - 1),
               reads=(b_v2, b_tabs), writes=(b_PB[2],), tag="m2mm")
            mm(PB[2][32:33, :], onesc_sb[:], v2sq_buf[sq][:],
               start=(dout == 0), stop=(dout == NC - 1),
               reads=(b_v2sq, b_tabs), writes=(b_PB[2],), tag="sq2mm")

        def ln2_rows():
            em.op("act", lambda e: e.activation(out=rows_sb[0:1, 0:512], in_=PB[2][0:1, :],
                                                func=AF.Copy, scale=1.0 / 2048.0),
                  reads=(b_PB[2],), writes=(b_rows,), tag="m2")
            em.op("act", lambda e: e.activation(out=rows_sb[0:1, _rowA:_rowA + 512],
                                                in_=rows_sb[0:1, 0:512], func=AF.Square),
                  reads=(b_rows,), writes=(b_rows,), tag="m2sq")
            em.op("dve",
                  lambda e: e.scalar_tensor_tensor(rows_sb[0:1, _rowB:_rowB + 512],
                                                   PB[2][32:33, :],
                                                   1.0 / 2048.0, rows_sb[0:1, _rowA:_rowA + 512],
                                                   op0=ALU.mult, op1=ALU.subtract),
                  reads=(b_PB[2], b_rows), writes=(b_rows,), tag="var")
            em.op("dve", lambda e: e.tensor_single_scalar(rows_sb[0:1, _rowB:_rowB + 512],
                                                          rows_sb[0:1, _rowB:_rowB + 512],
                                                          1.0e-5, ALU.add),
                  reads=(b_rows,), writes=(b_rows,), tag="epsadd")
            em.op("act", lambda e: e.activation(out=rows_sb[0:1, _rowA:_rowA + 512],
                                                in_=rows_sb[0:1, _rowB:_rowB + 512],
                                                func=AF.Sqrt),
                  reads=(b_rows,), writes=(b_rows,), tag="sd")
            em.op("dve", lambda e: e.reciprocal(rows_sb[0:1, _rowB:_rowB + 512],
                                                rows_sb[0:1, _rowA:_rowA + 512]),
                  reads=(b_rows,), writes=(b_rows,), tag="rstd")
            em.op("dve", lambda e: e.tensor_mul(rows_sb[0:1, _rowA:_rowA + 512],
                                                rows_sb[0:1, 0:512],
                                                rows_sb[0:1, _rowB:_rowB + 512]),
                  reads=(b_rows,), writes=(b_rows,), tag="nmr0")
            mm(PB[4][:], onesrf_sb[:], rows_sb[0:1, _rowB:_rowB + 512], start=True, stop=True,
               reads=(b_rows, b_tabs), writes=(b_PB[4],), tag="rstdB")

        def ap1(dout):
            em.op("dve",
                  lambda e, dout=dout: e.tensor_tensor(
                      v2_sb[:, 512 * dout:512 * (dout + 1)],
                      v2_sb[:, 512 * dout:512 * (dout + 1)], PB[4][:], op=ALU.mult),
                  reads=(b_PB[4], b_v2), writes=(b_v2,), tag="ap1")

        def nmrB():
            mm(PB[4][:], onesrf_sb[:], rows_sb[0:1, _rowA:_rowA + 512], start=True, stop=True,
               reads=(b_rows, b_tabs), writes=(b_PB[4],), tag="nmrB")

        def ap2(dout):
            em.op("dve",
                  lambda e, dout=dout: e.tensor_tensor(
                      v2_sb[:, 512 * dout:512 * (dout + 1)],
                      v2_sb[:, 512 * dout:512 * (dout + 1)],
                      PB[4][:], op=ALU.subtract),
                  reads=(b_PB[4], b_v2), writes=(b_v2,), tag="ap2")

        def y_store():
            dst = y_fm[b, :, 512 * g:512 * (g + 1)].rearrange("(dc p) t -> p dc t", p=128)
            src_ = v2_sb[:].rearrange("p (dc t) -> p dc t", t=512)
            dma("act", dst, src_, reads=(b_v2,), writes=(b_out,), tag="y_st")

        em.push(lambda: em.fence(b_v2, [b_k2, b_kn2] + b_cvtf))
        if b == 0 and g == 0:
            em.push(lambda: em.fence(b_dnstage, b_pu))
            em.push(lambda: dn_load(0, 0))
            em.push(lambda: dn_load(0, 1))
        else:
            dn_load(0, 0)
            dn_load(0, 1)
        for dout in range(NC):
            for q in range(4):
                em.push(lambda dout=dout, q=q: dn_quarter(dout, q))
            em.push(lambda dout=dout: dn_tail(dout))
        em.push(ln2_rows)
        for dout in range(NC):
            em.push(lambda dout=dout: ap1(dout))
        em.push(nmrB)
        for dout in range(NC):
            em.push(lambda dout=dout: ap2(dout))
        em.push(y_store)

    for b in range(B_loc):
        p1_with_p2g0(b)
        for g in range(G):
            p3(b, g)
            push_up(b, g)
            if g + 1 < G:
                for c in range(16):
                    em.fence(b_pu[c], [b_dnstage])
                for h in range(NH):
                    p2head(b, g + 1, h)
                em.drain()
            else:
                em.drain()
            push_down_ln2(b, g)
    em.drain()

    with nc.Block() as block:
        em.resolve_and_emit(nc, block)
    return nc


def _host_stage(inputs, B_loc, S):
    x = np.asarray(inputs["L0_attn_x"], dtype=np.float32)
    x_fm = np.ascontiguousarray(x.transpose(0, 2, 1))
    wqkv = np.ascontiguousarray(np.asarray(inputs["L0_attn_W_qkv"], np.float32))
    wproj = np.ascontiguousarray(np.asarray(inputs["L0_attn_W_proj"], np.float32))
    upT = np.ascontiguousarray(np.asarray(inputs["L0_ffn_up_weight"], np.float32).T)
    dnT = np.ascontiguousarray(np.asarray(inputs["L0_ffn_down_weight"], np.float32).T)
    bqkv = np.asarray(inputs["L0_attn_b_qkv"], np.float32)
    bqkv_t = np.ascontiguousarray(bqkv.reshape(18, 128).T)
    bv_row = np.ascontiguousarray(bqkv[H + HD:].reshape(1, 128))
    bproj_t = np.ascontiguousarray(
        np.asarray(inputs["L0_attn_b_proj"], np.float32).reshape(NC, 128).T)
    us_t = np.ascontiguousarray(
        np.asarray(inputs["L0_ffn_up_scale"], np.float32).reshape(HC, 128).T)
    ds_t = np.ascontiguousarray(
        np.asarray(inputs["L0_ffn_down_scale"], np.float32).reshape(NC, 128).T)
    os_col = np.ascontiguousarray(
        np.broadcast_to(np.asarray(inputs["L0_ffn_output_scale"], np.float32).reshape(1, 1),
                        (128, 1)))
    BF = ml_dtypes.bfloat16
    utri = np.triu(np.ones((128, 128), np.float32), 1).astype(BF)
    negi = (NEG_BIG * np.eye(128, dtype=np.float32)).astype(BF)
    shared = dict(wqkv=wqkv, wproj=wproj, upT=upT, dnT=dnT, bqkv_t=bqkv_t,
                  bv_row=bv_row, bproj_t=bproj_t, us_t=us_t, ds_t=ds_t, os_col=os_col,
                  utri_in=utri, negi_in=negi,
                  onesc_in=np.ones((128, 1), BF),
                  onescf_in=np.ones((128, 1), np.float32),
                  onesr_in=np.ones((1, 128), BF),
                  onesrf_in=np.ones((1, 128), np.float32))
    n_cores = x.shape[0] // B_loc
    in_maps = []
    for i in range(n_cores):
        m = dict(shared)
        m["x_fm"] = np.ascontiguousarray(x_fm[i * B_loc:(i + 1) * B_loc, :, :S])
        in_maps.append(m)
    return in_maps


def kernel(**inputs):
    B_loc, S = 2, 2048
    in_maps = _host_stage(inputs, B_loc, S)
    nc = build_program(B_loc, S)
    res = run_bass_kernel_spmd(nc, in_maps, core_ids=list(range(8)))
    shards = [np.asarray(r["y_fm"]) for r in res.results]
    y_fm = np.concatenate(shards, axis=0)
    return np.ascontiguousarray(y_fm.transpose(0, 2, 1)).astype(np.float32)
